# version 02
# baseline (speedup 1.0000x reference)
import numpy as np
import ml_dtypes
from contextlib import ExitStack

import concourse.bass as bass
import concourse.mybir as mybir
from concourse.bass_utils import run_bass_kernel_spmd

F32 = mybir.dt.float32
BF16 = mybir.dt.bfloat16
AF = mybir.ActivationFunctionType
ALU = mybir.AluOpType

P = 128
NB = 8
L = 4096
CDIM = 1024
H = 16
DH = 64
S = 512
NT = S // P
CB = CDIM // P
PRS = 8
JTN = L // P
NCORE = 8
NGRP = 4
EXP_SHIFT = -4.0
EPS = float(np.float32(1e-5))

_start = 2.0 ** (-2.0 ** (-(np.log2(H) - 3)))
SLOPES = [float(np.float32(_start * _start ** h)) for h in range(H)]
QSCALE = [1.0 / (8.0 * s) for s in SLOPES]

ENGINES = ("sync", "gpsimd", "tensor", "vector", "scalar")


class Bld:

    def __init__(self):
        self.streams = {e: [] for e in ENGINES}
        self.cnt = {}
        self.lastwait = {e: {} for e in ENGINES}
        self.sems = {}

    def inc(self, sem, n=1):
        self.cnt[sem] = self.cnt.get(sem, 0) + n
        return self.cnt[sem]

    def val(self, sem):
        return self.cnt.get(sem, 0)

    def em(self, e, fn):
        self.streams[e].append(fn)

    def wait(self, e, sem, v):
        if not v or v <= 0:
            return
        lw = self.lastwait[e]
        if lw.get(sem, 0) >= v:
            return
        lw[sem] = v
        self.em(e, lambda eng, s=sem, vv=v: eng.wait_ge(self.sems[s], vv))

    def op(self, e, sem, fn):
        v = self.inc(sem)
        self.em(e, lambda eng, f=fn, s=sem: f(eng).then_inc(self.sems[s], 1))
        return v

    def dma(self, e, sem, out_ap, in_ap):
        v = self.inc(sem, 16)
        self.em(e, lambda eng, o=out_ap, i=in_ap, s=sem:
                eng.dma_start(out=o, in_=i).then_inc(self.sems[s], 16))
        return v


def build_nc():
    nc = bass.Bass("TRN2", target_bir_lowering=False, debug=False)
    b = Bld()

    xs = nc.dram_tensor("xs", [NB, S, CDIM], F32, kind="ExternalInput").ap()
    win = {}
    for name in ("wqT", "wkT", "wvT", "woT", "wfT", "ufT", "wiT", "uiT",
                 "wcT", "ucT", "fwoT", "fuoT"):
        win[name] = nc.dram_tensor(name, [CDIM, CDIM], F32, kind="ExternalInput").ap()
    bias_in = {nm: nc.dram_tensor(nm, [P, CB], F32, kind="ExternalInput").ap()
               for nm in ("boat", "bfr", "bir", "bcr", "bor", "incr")}
    dtab_in = nc.dram_tensor("dtab", [P, JTN * S], BF16, kind="ExternalInput").ap()
    idbf_in = nc.dram_tensor("idbf", [P, P], BF16, kind="ExternalInput").ap()
    idf_in = nc.dram_tensor("idf", [P, P], F32, kind="ExternalInput").ap()
    ones_in = nc.dram_tensor("onesr", [P, DH], F32, kind="ExternalInput").ap()
    ppq_in = nc.dram_tensor("ppq", [P, PRS], F32, kind="ExternalInput").ap()
    NCON = 24
    cns_in = nc.dram_tensor("cns", [P, NCON], F32, kind="ExternalInput").ap()
    out = nc.dram_tensor("out", [NB, S, CDIM], F32, kind="ExternalOutput").ap()

    xnt_h = nc.dram_tensor("xnt_h", [NB, P, L], BF16)
    qt_h = nc.dram_tensor("qt_h", [NB, P, L], BF16)
    kcon = nc.dram_tensor("kcon", [64, 65536], BF16)
    kg = nc.dram_tensor("kg", [NCORE * 64, 65536], BF16)
    vcon = nc.dram_tensor("vcon", [L, H * 65], BF16)
    vg = nc.dram_tensor("vg", [NCORE * L, H * 65], BF16)
    attn_h = nc.dram_tensor("attn_h", [NB, P, L], F32)

    with ExitStack() as top:
        for s_ in ("qSY", "qGP", "cc", "tEV", "vEV", "sEV", "qOUT"):
            b.sems[s_] = top.enter_context(nc.semaphore(s_))

        sb = lambda name, shp, dt: top.enter_context(nc.sbuf_tensor(name, shp, dt))

        cns_sb = sb("cns_sb", [P, NCON], F32)
        bias_sb = {nm: sb(nm + "_sb", [P, CB], F32) for nm in
                   ("boat", "bfr", "bir", "bcr", "bor", "incr")}
        ppq_sb = sb("ppq_sb", [P, PRS], F32)
        ones_sb = sb("ones_sb", [P, DH], F32)
        idbf_sb = sb("idbf_sb", [P, P], BF16)
        idf_sb = sb("idf_sb", [P, P], F32)
        cT = sb("cT", [P, CB * S], F32)
        cbf = [sb("cbf0", [P, CB * S], BF16), sb("cbf1", [P, CB * S], BF16)]

        cvals = {}

        def constv(v):
            v = float(v)
            if v not in cvals:
                i = len(cvals)
                assert i < NCON
                cvals[v] = i
                nc.const_aps.register(F32, v, cns_sb[:, i:i + 1])
            return v

        for s_ in SLOPES:
            constv(s_)
        constv(EXP_SHIFT)
        constv(EPS)
        constv(-0.5)

        pst = [top.enter_context(nc.psum_tensor("pg%d" % i, [P, 2 * S], F32))
               for i in range(4)]
        ps = [pst[i // 2][:, (i % 2) * S:(i % 2 + 1) * S] for i in range(8)]
        ps_bf = [pst[i // 2].ap().bitcast(BF16)[:, (i % 2) * 2 * S:((i % 2) * 2 + 2) * S]
                 for i in range(8)]
        avp = [pst[2][0:65, 0:S], pst[2][0:65, S:2 * S],
               pst[3][0:65, 0:S], pst[3][0:65, S:2 * S]]
        avdn = [pst[2][DH:DH + 1, 0:S], pst[2][DH:DH + 1, S:2 * S],
                pst[3][DH:DH + 1, 0:S], pst[3][DH:DH + 1, S:2 * S]]
        avrep = [pst[2][DH:P, 0:S], pst[2][DH:P, S:2 * S],
                 pst[3][DH:P, 0:S], pst[3][DH:P, S:2 * S]]

        for nm, src in [("cns", cns_in), ("ppq", ppq_in), ("onesr", ones_in),
                        ("idbf", idbf_in), ("idf", idf_in)]:
            dst = {"cns": cns_sb, "ppq": ppq_sb, "onesr": ones_sb,
                   "idbf": idbf_sb, "idf": idf_sb}[nm]
            b.dma("sync", "qSY", dst[:, :], src)
        for nm in bias_sb:
            b.dma("sync", "qSY", bias_sb[nm][:, :], bias_in[nm])
        pre_ld = b.val("qSY")
        for e in ENGINES:
            b.wait(e, "qSY", pre_ld)

        with ExitStack() as p0:
            s0 = lambda name, shp, dt: p0.enter_context(nc.sbuf_tensor(name, shp, dt))
            wq_sb = s0("wq_sb", [P, CB * CDIM], BF16)
            wk_sb = s0("wk_sb", [P, CB * CDIM], BF16)
            wv_sb = s0("wv_sb", [P, CB * CDIM], BF16)
            xnt_sb = [s0("xnt0", [P, L], BF16), s0("xnt1", [P, L], BF16)]
            kc_sb = s0("kc_sb", [P, L], BF16)
            qt_sb = s0("qt_sb", [P, L], BF16)
            vc_sb = [s0("vc0", [P, H * 65], BF16), s0("vc1", [P, H * 65], BF16)]
            xt_sb = [s0("xt0", [P, CDIM], F32), s0("xt1", [P, CDIM], F32)]
            xnb_sb = [s0("xnb0", [P, CDIM], BF16), s0("xnb1", [P, CDIM], BF16)]
            st_sb = s0("st_sb", [P, 12], F32)
            ag_sb = s0("ag_sb", [P, 2], F32)
            nm_sb = s0("nm_sb", [P, 1], F32)
            rs_sb = s0("rs_sb", [P, 1], F32)

            for dst, src in ((wq_sb, win["wqT"]), (wk_sb, win["wkT"]), (wv_sb, win["wvT"])):
                b.dma("gpsimd", "qGP", dst.rearrange("p (cb d) -> p cb d", d=CDIM),
                      src.rearrange("(cb p) d -> p cb d", p=P))
            wqkv_ld = b.val("qGP")

            ones_ev = 0
            for buf in vc_sb:
                ap = buf.rearrange("p (h d) -> p h d", d=65)[:, :, 64:65]
                ones_ev = b.op("vector", "vEV", lambda v, a=ap: v.memset(a, 1.0))

            ln_done = {}
            tp_last = {}
            tp_bank = {}
            xn_all = {}
            kbank = {}
            kcq_dma = {}
            xnt_dma = {}
            vdma = {}
            prod_end = {}

            for t in range(NB):
                par = t % 2
                for tt in range(NT):
                    g = t * NT + tt
                    tp = g % 2
                    b.wait("sync", "vEV", ln_done.get(g - 2, 0))
                    xld = b.dma("sync", "qSY", xt_sb[tp][:, :], xs[t, tt * P:(tt + 1) * P, :])
                    b.wait("vector", "qSY", xld)
                    v1 = b.op("vector", "vEV", lambda v, x=xt_sb[tp]: v.bn_stats(st_sb[:, 0:6], x[:, 0:512]))
                    v2 = b.op("vector", "vEV", lambda v, x=xt_sb[tp]: v.bn_stats(st_sb[:, 6:12], x[:, 512:1024]))
                    b.wait("vector", "vEV", v2)
                    v3 = b.op("vector", "vEV", lambda v: v.bn_aggr(ag_sb[:, 0:2], st_sb[:, 0:12]))
                    b.wait("vector", "vEV", v3)
                    v4 = b.op("vector", "vEV", lambda v: v.tensor_scalar(nm_sb[:, :], ag_sb[:, 0:1], -1.0, None, ALU.mult))
                    b.wait("scalar", "vEV", v3)
                    s1 = b.op("scalar", "sEV", lambda s: s.activation(rs_sb[:, :], ag_sb[:, 1:2], AF.Ln, bias=constv(EPS)))
                    b.wait("scalar", "sEV", s1)
                    s2 = b.op("scalar", "sEV", lambda s: s.activation(rs_sb[:, :], rs_sb[:, :], AF.Exp, scale=constv(-0.5)))
                    b.wait("vector", "sEV", s2)
                    b.wait("vector", "vEV", v4)
                    b.wait("vector", "tEV", tp_last.get(g - 2, 0))
                    ln_done[g] = b.op("vector", "vEV",
                                      lambda v, x=xt_sb[tp], o=xnb_sb[tp]:
                                      v.tensor_scalar(o[:, :], x[:, :], nm_sb[:, :], rs_sb[:, :], ALU.add, ALU.mult))
                    for ci in range(CB):
                        bfv = ps_bf[4 + (ci % 2)]
                        b.wait("tensor", "vEV", ln_done[g])
                        b.wait("tensor", "vEV", tp_bank.get(ci % 2, 0))
                        tpv = b.op("tensor", "tEV",
                                   lambda te, src=xnb_sb[tp], c=ci, ov=bfv:
                                   te.transpose(ov[:, 0:P], src[:, c * P:(c + 1) * P], idbf_sb[:, :]))
                        b.wait("vector", "tEV", tpv)
                        ev = b.op("vector", "vEV",
                                  lambda v, ov=bfv, dst=xnt_sb[par], c=ci, tti=tt:
                                  v.tensor_copy(dst[:, c * S + tti * P:c * S + (tti + 1) * P], ov[:, 0:P]))
                        tp_bank[ci % 2] = ev
                        tp_last[g] = tpv
                    xn_all[t] = ev

                b.wait("sync", "vEV", xn_all[t])
                xnt_dma[t] = b.dma("sync", "qSY", xnt_h.ap()[t], xnt_sb[par][:, :])

                b.wait("tensor", "qGP", wqkv_ld)
                last_ev = 0
                for which in ("k", "q"):
                    wsb = wk_sb if which == "k" else wq_sb
                    for pr in range(PRS):
                        bank = ps[2 + (pr % 2)]
                        b.wait("tensor", "vEV", kbank.get(pr % 2, 0))
                        b.wait("tensor", "vEV", xn_all[t])
                        if which == "k" and pr == 0:
                            b.wait("vector", "qSY", kcq_dma.get(t - 1, 0))
                        for ci in range(CB):
                            tv = b.op("tensor", "tEV",
                                      lambda te, w=wsb, pp=pr, c=ci, bk=bank, xp=xnt_sb[par]:
                                      te.matmul(bk, w[:, c * CDIM + pp * P:c * CDIM + (pp + 1) * P],
                                                xp[:, c * S:(c + 1) * S],
                                                start=(c == 0), stop=(c == CB - 1)))
                        b.wait("vector", "tEV", tv)
                        if which == "k":
                            ev = b.op("vector", "vEV",
                                      lambda v, bk=bank, pp=pr: v.tensor_copy(kc_sb[:, pp * S:(pp + 1) * S], bk))
                        else:
                            ev = b.op("vector", "vEV",
                                      lambda v, bk=bank, pp=pr:
                                      v.tensor_scalar(qt_sb[:, pp * S:(pp + 1) * S], bk,
                                                      ppq_sb[:, pp:pp + 1], None, ALU.mult))
                        kbank[pr % 2] = ev
                        last_ev = ev

                b.wait("sync", "vEV", last_ev)
                b.dma("sync", "qSY", qt_h.ap()[t], qt_sb[:, :])
                kview = kcon.rearrange("r (p h j) -> r p h j", p=DH, h=PRS)
                for jtl in range(NT):
                    for hpar in range(2):
                        row = t * 8 + jtl * 2 + hpar
                        src = kc_sb[hpar * DH:(hpar + 1) * DH, :]\
                            .rearrange("p (h tok) -> p h tok", tok=S)[:, :, jtl * P:(jtl + 1) * P]
                        b.dma("sync", "qSY", kview[row], src)
                kcq_dma[t] = b.val("qSY")

                b.wait("vector", "vEV", ones_ev)
                for tt in range(NT):
                    gv = t * NT + tt
                    vpar = gv % 2
                    for dc in range(2):
                        bank = ps[2 + dc]
                        b.wait("tensor", "vEV", kbank.get(dc, 0))
                        if dc == 0:
                            b.wait("tensor", "qSY", vdma.get(gv - 2, 0))
                        for ci in range(CB):
                            tv = b.op("tensor", "tEV",
                                      lambda te, c=ci, tti=tt, dci=dc, bk=bank, xp=xnt_sb[par]:
                                      te.matmul(bk,
                                                xp[:, c * S + tti * P:c * S + (tti + 1) * P],
                                                wv_sb[:, c * CDIM + dci * S:c * CDIM + (dci + 1) * S],
                                                start=(c == 0), stop=(c == CB - 1)))
                        b.wait("vector", "tEV", tv)
                        b.wait("vector", "qSY", vdma.get(gv - 2, 0))
                        dstv = vc_sb[vpar].rearrange("p (h d) -> p h d", d=65)[:, dc * 8:(dc + 1) * 8, 0:DH]
                        srcv = bank.rearrange("p (h d) -> p h d", d=DH)
                        ev = b.op("vector", "vEV", lambda v, dd=dstv, ss=srcv: v.tensor_copy(dd, ss))
                        kbank[dc] = ev
                    b.wait("sync", "vEV", ev)
                    vdma[gv] = b.dma("sync", "qSY",
                                     vcon.ap()[t * S + tt * P:t * S + (tt + 1) * P, :],
                                     vc_sb[vpar][:, :])
                prod_end[t] = b.val("tEV")
            p0_t, p0_v, p0_s = b.val("tEV"), b.val("vEV"), b.val("sEV")
            p0_sy, p0_gp = b.val("qSY"), b.val("qGP")

        b.wait("gpsimd", "qSY", p0_sy)
        for con, gat in ((kcon, kg), (vcon, vg)):
            b.inc("cc", 1)
            b.em("gpsimd", lambda g, ci=con, go=gat: g.collective_compute(
                "AllGather", ALU.bypass, replica_groups=[list(range(NCORE))],
                ins=[ci.ap().opt()], outs=[go.ap().opt()]).then_inc(b.sems["cc"], 1))
        cc_done = b.val("cc")

        with ExitStack() as p1:
            s1_ = lambda name, shp, dt: p1.enter_context(nc.sbuf_tensor(name, shp, dt))
            wg_sb = {nm: s1_(nm + "_sb", [P, CB * CDIM], BF16)
                     for nm in ("wfT", "ufT", "wiT", "uiT", "wcT", "ucT")}
            xnf_sb = [s1_("xnf0", [P, L], BF16), s1_("xnf1", [P, L], BF16)]
            f_sb = s1_("f_sb", [P, S], F32)
            i_sb = s1_("i_sb", [P, S], F32)
            g_sb = s1_("g_sb", [P, S], F32)
            t1_sb = s1_("t1_sb", [P, S], F32)
            t2_sb = s1_("t2_sb", [P, S], F32)

            for qe in ("sync", "gpsimd"):
                b.wait(qe, "tEV", p0_t)
                b.wait(qe, "vEV", p0_v)
                b.wait(qe, "qSY", p0_sy)
                b.wait(qe, "qGP", p0_gp)
            for qe in ("tensor", "vector", "scalar"):
                b.wait(qe, "tEV", p0_t)
                b.wait(qe, "vEV", p0_v)
                b.wait(qe, "sEV", p0_s)
            for nm in ("wfT", "ufT", "wiT", "uiT", "wcT", "ucT"):
                b.dma("gpsimd", "qGP", wg_sb[nm].rearrange("p (cb d) -> p cb d", d=CDIM),
                      win[nm].rearrange("(cb p) d -> p cb d", p=P))
            wffn_ld = b.val("qGP")
            xnf_ld = {}
            b.wait("sync", "qSY", xnt_dma[0])
            xnf_ld[0] = b.dma("sync", "qSY", xnf_sb[0][:, :], xnt_h.ap()[0])

            vz = b.op("vector", "vEV", lambda v: v.memset(cT[:, :], 0.0))
            b.wait("scalar", "vEV", vz)
            c0e = 0
            for mt in range(CB):
                c0e = b.op("scalar", "sEV",
                           lambda s, m=mt: s.activation(cT[:, m * S:(m + 1) * S], cT[:, m * S:(m + 1) * S],
                                                        AF.Identity, bias=bias_sb["incr"][:, m:m + 1]))
            b.wait("vector", "sEV", c0e)
            cb0 = b.op("vector", "vEV", lambda v: v.tensor_copy(cbf[0][:, :], cT[:, :]))

            gate_specs = [("wfT", "ufT", "bfr", AF.Sigmoid, f_sb),
                          ("wiT", "uiT", "bir", AF.Sigmoid, i_sb),
                          ("wcT", "ucT", "bcr", AF.Tanh, g_sb)]
            gact = {}
            gbuf_used = {}
            ffn_t_end = {}
            last_cupd = cb0
            for t in range(NB):
                rpar, wpar = t % 2, (t + 1) % 2
                if t + 1 < NB:
                    b.wait("sync", "tEV", ffn_t_end.get(t - 1, 0))
                    b.wait("sync", "qSY", xnt_dma[t + 1])
                    xnf_ld[t + 1] = b.dma("sync", "qSY", xnf_sb[wpar][:, :], xnt_h.ap()[t + 1])
                for mt in range(CB):
                    gevs = []
                    for gi, (wn, un, bn, fn, gbuf) in enumerate(gate_specs):
                        bank = ps[gi]
                        b.wait("tensor", "qGP", wffn_ld)
                        b.wait("tensor", "qSY", xnf_ld[t])
                        b.wait("tensor", "vEV", last_cupd)
                        b.wait("tensor", "sEV", gact.get(gi, 0))
                        for ci in range(CB):
                            b.op("tensor", "tEV",
                                 lambda te, w=wg_sb[wn], m=mt, c=ci, bk=bank, xp=xnf_sb[rpar]:
                                 te.matmul(bk, w[:, c * CDIM + m * P:c * CDIM + (m + 1) * P],
                                           xp[:, c * S:(c + 1) * S], start=(c == 0), stop=False))
                        for ci in range(CB):
                            tv = b.op("tensor", "tEV",
                                      lambda te, u=wg_sb[un], m=mt, c=ci, bk=bank, cc_=cbf[rpar]:
                                      te.matmul(bk, u[:, c * CDIM + m * P:c * CDIM + (m + 1) * P],
                                                cc_[:, c * S:(c + 1) * S], start=False, stop=(c == CB - 1)))
                        b.wait("scalar", "tEV", tv)
                        b.wait("scalar", "vEV", gbuf_used.get(gi, 0))
                        sv = b.op("scalar", "sEV",
                                  lambda s, bk=bank, f=fn, bb=bn, m=mt, gb=gbuf:
                                  s.activation(gb[:, :], bk, f, bias=bias_sb[bb][:, m:m + 1]))
                        gact[gi] = sv
                        gevs.append(sv)
                    b.wait("vector", "sEV", gevs[0])
                    u1 = b.op("vector", "vEV",
                              lambda v, m=mt: v.tensor_tensor(t1_sb[:, :], f_sb[:, :], cT[:, m * S:(m + 1) * S], ALU.mult))
                    b.wait("vector", "sEV", gevs[2])
                    u2 = b.op("vector", "vEV",
                              lambda v: v.tensor_tensor(t2_sb[:, :], i_sb[:, :], g_sb[:, :], ALU.mult))
                    b.wait("vector", "vEV", u2)
                    u3 = b.op("vector", "vEV",
                              lambda v, m=mt: v.tensor_tensor(cT[:, m * S:(m + 1) * S], t1_sb[:, :], t2_sb[:, :], ALU.add))
                    b.wait("vector", "vEV", u3)
                    u4 = b.op("vector", "vEV",
                              lambda v, m=mt, wp=wpar: v.tensor_copy(cbf[wp][:, m * S:(m + 1) * S],
                                                                     cT[:, m * S:(m + 1) * S]))
                    gbuf_used[0] = u1
                    gbuf_used[1] = u2
                    gbuf_used[2] = u2
                    last_cupd = u4
                ffn_t_end[t] = b.val("tEV")
            cbf_fin = cbf[NB % 2]
            p1_t, p1_v, p1_s = b.val("tEV"), b.val("vEV"), b.val("sEV")
            p1_sy, p1_gp = b.val("qSY"), b.val("qGP")

        with ExitStack() as p2:
            s2_ = lambda name, shp, dt: p2.enter_context(nc.sbuf_tensor(name, shp, dt))
            dtab_sb = s2_("dtab_sb", [P, JTN * S], BF16)
            wo64_sb = s2_("wo64_sb", [DH, H * CDIM], BF16)
            KTb = [s2_("KT0", [P, 16 * 256], BF16), s2_("KT1", [P, 16 * 256], BF16)]
            VTb = [s2_("VT0", [P, 16 * 260], BF16), s2_("VT1", [P, 16 * 260], BF16)]
            QT = [s2_("QT0", [P, L], BF16)]
            tsub = [s2_("ts%d" % i, [P, S], F32) for i in range(4)]
            Pt = [s2_("Pt%d" % i, [P, S], BF16) for i in range(4)]
            AVT = [s2_("AVT0", [DH, H * S], BF16)]
            norm_sb = s2_("norm_sb", [P, S], F32)
            rep_sb = s2_("rep_sb", [DH, S], F32)
            attn_sb = [s2_("at0", [P, L], F32)]

            for qe in ("sync", "gpsimd"):
                b.wait(qe, "tEV", p1_t)
                b.wait(qe, "vEV", p1_v)
                b.wait(qe, "sEV", p1_s)
                b.wait(qe, "qSY", p1_sy)
                b.wait(qe, "qGP", p1_gp)
            for qe in ("tensor", "vector", "scalar"):
                b.wait(qe, "tEV", p1_t)
                b.wait(qe, "vEV", p1_v)
                b.wait(qe, "sEV", p1_s)
            b.wait("sync", "cc", cc_done)
            dtab_ld = b.dma("sync", "qSY", dtab_sb[:, :], dtab_in)
            for h in range(H):
                b.dma("gpsimd", "qGP", wo64_sb[:, h * CDIM:(h + 1) * CDIM],
                      win["woT"][h * DH:(h + 1) * DH, :])
            wo_ld = b.val("qGP")

            kgr = kg.rearrange("(c bb jl hp) (p x) -> c bb jl hp p x", c=NCORE, bb=NB, jl=NT, p=DH)
            vgr = vg.rearrange("(c bb jl p) x -> c bb jl p x", c=NCORE, bb=NB, jl=NT)

            def load_kv(bb, hg, half, buf):
                for hpar in range(2):
                    for cl in range(4):
                        core = 4 * half + cl
                        src = kgr[core, bb, :, hpar, :, 2 * hg * P:2 * hg * P + 256]\
                            .rearrange("jl p x -> p jl x")
                        dst = KTb[buf][hpar * DH:(hpar + 1) * DH, cl * 1024:(cl + 1) * 1024]\
                            .rearrange("p (jl x) -> p jl x", x=256)
                        b.dma("sync", "qSY", dst, src)
                for cl in range(4):
                    core = 4 * half + cl
                    src = vgr[core, bb, :, :, 4 * hg * 65:(4 * hg + 4) * 65]\
                        .rearrange("jl p x -> p jl x")
                    dst = VTb[buf][:, cl * 1040:(cl + 1) * 1040]\
                        .rearrange("p (jl x) -> p jl x", x=260)
                    b.dma("sync", "qSY", dst, src)
                return b.val("qSY")

            sub_done = [0, 0, 0, 0]
            exp_done = [0, 0, 0, 0]
            av_done = [0, 0, 0, 0]
            kv_free = [0, 0]
            kv_ld = {}
            avt_mult = {}
            rep_state = {"rpm": 0, "evac": 0, "mul": 0}
            st_lastq = {}
            proj_done = {}
            attn_dma = {}
            act_bank = {}
            act_last = {}
            nstep = 0
            pend = []

            def flush_av(limit):
                while len(pend) > limit:
                    par, hl, jt, jl2, buf, sexp = pend.pop(0)
                    b.wait("tensor", "sEV", sexp)
                    if jt == 0:
                        b.wait("tensor", "vEV", avt_mult.get(hl, 0))
                    v_av = b.op("tensor", "tEV",
                                lambda te, pp=par, hli=hl, bf_=buf, jj=jl2,
                                first=(jt == 0), last=(jt == JTN - 1):
                                te.matmul(avp[hli],
                                          VTb[bf_][:, jj * 260 + hli * 65:jj * 260 + (hli + 1) * 65],
                                          Pt[pp][:, :], start=first, stop=last))
                    av_done[par] = v_av
                    kv_free[buf] = v_av

            NCHUNK = NB * NGRP * 2
            for bb in range(NB):
                b.wait("sync", "tEV", st_lastq.get(bb - 1, 0))
                qld = b.dma("sync", "qSY", QT[0][:, :], qt_h.ap()[bb])
                for hg in range(NGRP):
                    g2 = bb * NGRP + hg
                    if g2 == 0:
                        kv_ld[0] = load_kv(0, 0, 0, 0)
                    for jt in range(JTN):
                        if jt in (1, 17):
                            nxt = 2 * g2 + (1 if jt == 1 else 2)
                            if nxt < NCHUNK and nxt not in kv_ld:
                                cbb, rem = divmod(nxt, 2 * NGRP)
                                chg, chalf = divmod(rem, 2)
                                b.wait("sync", "tEV", kv_free[nxt % 2])
                                kv_ld[nxt] = load_kv(cbb, chg, chalf, nxt % 2)
                        half = jt // 16
                        jl2 = jt % 16
                        c2 = 2 * g2 + half
                        for hl in range(4):
                            h = hg * 4 + hl
                            par = nstep % 4
                            hpar = h % 2
                            prl = hl // 2
                            b.wait("tensor", "vEV", sub_done[par])
                            if par < 2:
                                b.wait("tensor", "sEV", act_bank.get(par, 0))
                            b.wait("tensor", "qSY", kv_ld[c2])
                            b.wait("tensor", "qSY", qld)
                            v_st = b.op("tensor", "tEV",
                                        lambda te, pp=par, hp=hpar, pl=prl, hh=h, jj=jl2, bf_=c2 % 2:
                                        te.matmul(ps[pp],
                                                  KTb[bf_][hp * DH:(hp + 1) * DH, jj * 256 + pl * P:jj * 256 + (pl + 1) * P],
                                                  QT[0][hp * DH:(hp + 1) * DH, (hh // 2) * S:(hh // 2 + 1) * S],
                                                  start=True, stop=True))
                            flush_av(1)
                            b.wait("vector", "tEV", v_st)
                            b.wait("vector", "sEV", exp_done[par])
                            b.wait("vector", "qSY", dtab_ld)
                            v_sub = b.op("vector", "vEV",
                                         lambda v, pp=par, jj=jt:
                                         v.tensor_tensor(tsub[pp][:, :], ps[pp],
                                                         dtab_sb[:, jj * S:(jj + 1) * S], ALU.subtract))
                            sub_done[par] = v_sub
                            b.wait("scalar", "vEV", v_sub)
                            b.wait("scalar", "tEV", av_done[par])
                            v_exp = b.op("scalar", "sEV",
                                         lambda s, pp=par, hh=h:
                                         s.activation(Pt[pp][:, :], tsub[pp][:, :], AF.Exp,
                                                      bias=constv(EXP_SHIFT), scale=constv(SLOPES[hh])))
                            exp_done[par] = v_exp
                            pend.append((par, hl, jt, jl2, c2 % 2, v_exp))
                            nstep += 1
                    flush_av(0)
                    grp_av = b.val("tEV")
                    for hl in range(4):
                        h = hg * 4 + hl
                        b.wait("vector", "tEV", grp_av)
                        b.wait("vector", "tEV", rep_state["rpm"])
                        rcp = b.op("vector", "vEV",
                                   lambda v, hli=hl: v.reciprocal(norm_sb[DH:DH + 1, :], avdn[hli]))
                        b.wait("tensor", "vEV", rcp)
                        b.wait("tensor", "vEV", rep_state["evac"])
                        rpm = b.op("tensor", "tEV",
                                   lambda te, hli=hl: te.matmul(avrep[hli], ones_sb[DH:DH + 1, 0:DH],
                                                                norm_sb[DH:DH + 1, :], start=True, stop=True,
                                                                tile_position=(DH, DH)))
                        rep_state["rpm"] = rpm
                        b.wait("vector", "tEV", rpm)
                        b.wait("vector", "vEV", rep_state["mul"])
                        rpe = b.op("vector", "vEV", lambda v, hli=hl: v.tensor_copy(rep_sb[:, :], avrep[hli]))
                        rep_state["evac"] = rpe
                        b.wait("vector", "vEV", rpe)
                        b.wait("vector", "tEV", proj_done.get(bb - 1, 0))
                        mul = b.op("vector", "vEV",
                                   lambda v, hli=hl, hh=h, bp=bb % 2:
                                   v.tensor_tensor(AVT[0][:, hh * S:(hh + 1) * S],
                                                   avp[hli][0:DH, :], rep_sb[:, :], ALU.mult))
                        rep_state["mul"] = mul
                        avt_mult[hl] = mul
                st_lastq[bb] = b.val("tEV")
                allmul = b.val("vEV")
                for ot in range(CB):
                    bank = ps[ot % 2]
                    b.wait("tensor", "vEV", allmul)
                    b.wait("tensor", "qGP", wo_ld)
                    b.wait("tensor", "sEV", act_bank.get(ot % 2, 0))
                    for h in range(H):
                        tv = b.op("tensor", "tEV",
                                  lambda te, hh=h, o=ot, bk=bank, bp=bb % 2:
                                  te.matmul(bk, wo64_sb[:, hh * CDIM + o * P:hh * CDIM + (o + 1) * P],
                                            AVT[0][:, hh * S:(hh + 1) * S],
                                            start=(hh == 0), stop=(hh == H - 1)))
                    b.wait("scalar", "tEV", tv)
                    b.wait("scalar", "qSY", attn_dma.get(bb - 1, 0))
                    sv = b.op("scalar", "sEV",
                              lambda s, bk=bank, o=ot, bp=bb % 2:
                              s.activation(attn_sb[0][:, o * S:(o + 1) * S], bk,
                                           AF.Identity, bias=bias_sb["boat"][:, o:o + 1]))
                    act_bank[ot % 2] = sv
                proj_done[bb] = b.val("tEV")
                act_last[bb] = b.val("sEV")
                b.wait("sync", "sEV", act_last[bb])
                attn_dma[bb] = b.dma("sync", "qSY", attn_h.ap()[bb], attn_sb[0][:, :])
            p2_t, p2_v, p2_s = b.val("tEV"), b.val("vEV"), b.val("sEV")
            p2_sy, p2_gp = b.val("qSY"), b.val("qGP")

        with ExitStack() as p3:
            s3_ = lambda name, shp, dt: p3.enter_context(nc.sbuf_tensor(name, shp, dt))
            wo_sb = s3_("wo_sb", [P, CB * CDIM], BF16)
            uo_sb = s3_("uo_sb", [P, CB * CDIM], BF16)
            uoC_sb = s3_("uoC_sb", [P, CB * S], F32)
            xnf2 = [s3_("xnf2a", [P, L], BF16), s3_("xnf2b", [P, L], BF16)]
            att2 = [s3_("att2a", [P, L], F32), s3_("att2b", [P, L], F32)]
            t3_sb = [s3_("t3a", [P, S], F32), s3_("t3b", [P, S], F32)]
            ht_sb = [s3_("hta", [P, S], F32), s3_("htb", [P, S], F32)]
            sum_sb = [s3_("suma", [P, L], F32), s3_("sumb", [P, L], F32)]
            xnat = [s3_("xna", [P, CDIM], F32), s3_("xnb2", [P, CDIM], F32)]
            onat = [s3_("ona", [P, CDIM], F32), s3_("onb", [P, CDIM], F32)]

            for qe in ("sync", "gpsimd"):
                b.wait(qe, "tEV", p2_t)
                b.wait(qe, "vEV", p2_v)
                b.wait(qe, "sEV", p2_s)
                b.wait(qe, "qSY", p2_sy)
                b.wait(qe, "qGP", p2_gp)
            for qe in ("tensor", "vector", "scalar"):
                b.wait(qe, "tEV", p2_t)
                b.wait(qe, "vEV", p2_v)
                b.wait(qe, "sEV", p2_s)
            b.dma("gpsimd", "qGP", wo_sb.rearrange("p (cb d) -> p cb d", d=CDIM),
                  win["fwoT"].rearrange("(cb p) d -> p cb d", p=P))
            b.dma("gpsimd", "qGP", uo_sb.rearrange("p (cb d) -> p cb d", d=CDIM),
                  win["fuoT"].rearrange("(cb p) d -> p cb d", p=P))
            wP3_ld = b.val("qGP")

            uoC_ev = {}
            for mt in range(CB):
                bank = ps[mt % 2]
                b.wait("tensor", "qGP", wP3_ld)
                b.wait("tensor", "vEV", uoC_ev.get(mt - 2, 0))
                for ci in range(CB):
                    tv = b.op("tensor", "tEV",
                              lambda te, m=mt, c=ci, bk=bank:
                              te.matmul(bk, uo_sb[:, c * CDIM + m * P:c * CDIM + (m + 1) * P],
                                        cbf_fin[:, c * S:(c + 1) * S],
                                        start=(c == 0), stop=(c == CB - 1)))
                b.wait("vector", "tEV", tv)
                uoC_ev[mt] = b.op("vector", "vEV",
                                  lambda v, m=mt, bk=bank: v.tensor_copy(uoC_sb[:, m * S:(m + 1) * S], bk))
            uoC_all = uoC_ev[CB - 1]

            ld3 = {}
            sum_ev = {}
            t3_used = {}
            ht_used = {}
            wo_bank = {}
            tpo_done = {}
            out_dma = {}
            onat_ev = {}
            ob_bank = {}
            b.wait("sync", "qSY", xnt_dma[0])
            b.wait("sync", "qSY", attn_dma[0])
            ld3[0] = b.dma("sync", "qSY", att2[0][:, :], attn_h.ap()[0])
            b.dma("sync", "qSY", xnf2[0][:, :], xnt_h.ap()[0])
            ld3[0] = b.val("qSY")
            for t in range(NB):
                tp = t % 2
                if t + 1 < NB:
                    b.wait("sync", "tEV", tpo_done.get(t - 1, 0))
                    b.wait("sync", "vEV", sum_ev.get((t - 1, CB - 1), 0))
                    b.wait("sync", "qSY", attn_dma[t + 1])
                    b.dma("sync", "qSY", att2[(t + 1) % 2][:, :], attn_h.ap()[t + 1])
                    b.dma("sync", "qSY", xnf2[(t + 1) % 2][:, :], xnt_h.ap()[t + 1])
                    ld3[t + 1] = b.val("qSY")
                for mt in range(CB):
                    bank = ps[mt % 2]
                    b.wait("tensor", "qSY", ld3[t])
                    b.wait("tensor", "vEV", uoC_all)
                    b.wait("tensor", "vEV", wo_bank.get(mt % 2, 0))
                    for ci in range(CB):
                        tv = b.op("tensor", "tEV",
                                  lambda te, m=mt, c=ci, bk=bank, xp=xnf2[tp]:
                                  te.matmul(bk, wo_sb[:, c * CDIM + m * P:c * CDIM + (m + 1) * P],
                                            xp[:, c * S:(c + 1) * S],
                                            start=(c == 0), stop=(c == CB - 1)))
                    b.wait("vector", "tEV", tv)
                    b.wait("vector", "sEV", t3_used.get(mt % 2, 0))
                    v5 = b.op("vector", "vEV",
                              lambda v, m=mt, bk=bank, pp=mt % 2:
                              v.scalar_tensor_tensor(t3_sb[pp][:, :], bk, 1.0,
                                                     uoC_sb[:, m * S:(m + 1) * S], ALU.mult, ALU.add))
                    wo_bank[mt % 2] = v5
                    b.wait("scalar", "vEV", v5)
                    b.wait("scalar", "vEV", ht_used.get(mt % 2, 0))
                    s5 = b.op("scalar", "sEV",
                              lambda s, m=mt, pp=mt % 2:
                              s.activation(ht_sb[pp][:, :], t3_sb[pp][:, :], AF.Tanh,
                                           bias=bias_sb["bor"][:, m:m + 1]))
                    t3_used[mt % 2] = s5
                    b.wait("vector", "sEV", s5)
                    b.wait("vector", "qSY", ld3[t])
                    b.wait("vector", "tEV", tpo_done.get(t - 2, 0))
                    v6 = b.op("vector", "vEV",
                              lambda v, m=mt, pp=mt % 2, ap_=att2[tp], ss=sum_sb[tp]:
                              v.tensor_tensor(ss[:, m * S:(m + 1) * S], ht_sb[pp][:, :],
                                              ap_[:, m * S:(m + 1) * S], ALU.add))
                    ht_used[mt % 2] = v6
                    sum_ev[(t, mt)] = v6
                for tt in range(NT):
                    g = t * NT + tt
                    ttp = g % 2
                    b.wait("sync", "vEV", onat_ev.get(g - 2, 0))
                    xld3 = b.dma("sync", "qSY", xnat[ttp][:, :], xs[t, tt * P:(tt + 1) * P, :])
                    for ot in range(CB):
                        bank = ps[2 + (ot % 2)]
                        b.wait("tensor", "vEV", sum_ev[(t, ot)])
                        b.wait("tensor", "vEV", ob_bank.get(ot % 2, 0))
                        tpv = b.op("tensor", "tEV",
                                   lambda te, o=ot, tti=tt, bk=bank, ss=sum_sb[tp]:
                                   te.transpose(bk[:, 0:P], ss[:, o * S + tti * P:o * S + (tti + 1) * P],
                                                idf_sb[:, :]))
                        b.wait("vector", "tEV", tpv)
                        b.wait("vector", "qSY", xld3)
                        b.wait("vector", "qOUT", out_dma.get(g - 2, 0))
                        ov = b.op("vector", "vEV",
                                  lambda v, o=ot, bk=bank, tpp=ttp:
                                  v.scalar_tensor_tensor(onat[tpp][:, o * P:(o + 1) * P], bk[:, 0:P], 1.0,
                                                         xnat[tpp][:, o * P:(o + 1) * P], ALU.mult, ALU.add))
                        ob_bank[ot % 2] = ov
                        onat_ev[g] = ov
                    tpo_done[t] = b.val("tEV")
                    b.wait("sync", "vEV", onat_ev[g])
                    out_dma[g] = b.dma("sync", "qOUT", out[t, tt * P:(tt + 1) * P, :], onat[ttp][:, :])

            b.wait("sync", "qOUT", b.val("qOUT"))

        with nc.Block() as block:
            @block.sync
            def _(eng):
                for fn in b.streams["sync"]:
                    fn(eng)

            @block.gpsimd
            def _(eng):
                for fn in b.streams["gpsimd"]:
                    fn(eng)

            @block.tensor
            def _(eng):
                for fn in b.streams["tensor"]:
                    fn(eng)

            @block.vector
            def _(eng):
                for fn in b.streams["vector"]:
                    fn(eng)

            @block.scalar
            def _(eng):
                for fn in b.streams["scalar"]:
                    fn(eng)

    return nc


def _host_consts():
    idbf = np.eye(P, dtype=ml_dtypes.bfloat16)
    idf = np.eye(P, dtype=np.float32)
    onesr = np.ones((P, DH), dtype=np.float32)
    ppq = np.zeros((P, PRS), dtype=np.float32)
    for pr in range(PRS):
        ppq[0:DH, pr] = QSCALE[2 * pr]
        ppq[DH:P, pr] = QSCALE[2 * pr + 1]
    cns = np.zeros((P, 24), dtype=np.float32)
    vals = list(SLOPES) + [EXP_SHIFT, EPS, -0.5]
    for i, v in enumerate(vals):
        cns[:, i] = np.float32(v)
    return idbf, idf, onesr, ppq, cns


def _dtab_for_core(c):
    p = np.arange(P, dtype=np.float64)[:, None, None]
    jt = np.arange(JTN, dtype=np.float64)[None, :, None]
    f = np.arange(S, dtype=np.float64)[None, None, :]
    d = np.abs((c * S + f) - (jt * P + p))
    return d.reshape(P, JTN * S).astype(ml_dtypes.bfloat16)


def _reshape_bias(v):
    return np.ascontiguousarray(v.reshape(CB, P).T.astype(np.float32))


def kernel(L0_attn_q, L0_attn_Wq, L0_attn_Wk, L0_attn_Wv, L0_attn_Wo, L0_attn_bo,
           L0_ffn_wf, L0_ffn_uf, L0_ffn_bf, L0_ffn_wi, L0_ffn_ui, L0_ffn_bi,
           L0_ffn_wc, L0_ffn_uc, L0_ffn_bc, L0_ffn_wo, L0_ffn_uo, L0_ffn_bo,
           L0_ffn_in_c):
    x = np.asarray(L0_attn_q, dtype=np.float32)
    wT = {
        "wqT": L0_attn_Wq, "wkT": L0_attn_Wk, "wvT": L0_attn_Wv, "woT": L0_attn_Wo,
        "wfT": L0_ffn_wf, "ufT": L0_ffn_uf, "wiT": L0_ffn_wi, "uiT": L0_ffn_ui,
        "wcT": L0_ffn_wc, "ucT": L0_ffn_uc, "fwoT": L0_ffn_wo, "fuoT": L0_ffn_uo,
    }
    wT = {k: np.ascontiguousarray(np.asarray(v, dtype=np.float32).T) for k, v in wT.items()}
    biases = {
        "boat": _reshape_bias(np.asarray(L0_attn_bo, np.float32)),
        "bfr": _reshape_bias(np.asarray(L0_ffn_bf, np.float32)),
        "bir": _reshape_bias(np.asarray(L0_ffn_bi, np.float32)),
        "bcr": _reshape_bias(np.asarray(L0_ffn_bc, np.float32)),
        "bor": _reshape_bias(np.asarray(L0_ffn_bo, np.float32)),
        "incr": _reshape_bias(np.asarray(L0_ffn_in_c, np.float32)),
    }
    idbf, idf, onesr, ppq, cns = _host_consts()

    nc = build_nc()
    in_maps = []
    for c in range(NCORE):
        m = {"xs": np.ascontiguousarray(x[:, c * S:(c + 1) * S, :]),
             "dtab": _dtab_for_core(c),
             "idbf": idbf, "idf": idf, "onesr": onesr, "ppq": ppq, "cns": cns}
        m.update(wT)
        m.update(biases)
        in_maps.append(m)
    res = run_bass_kernel_spmd(nc, in_maps, core_ids=list(range(NCORE)))
    outs = [np.asarray(res.results[c]["out"]) for c in range(NCORE)]
    return np.concatenate(outs, axis=1)


if __name__ == "__main__":
    nc = build_nc()
    n = sum(len(v) for v in Bld().streams.values())
    print("built OK")


# version 06
# speedup vs baseline: 1.2841x; 1.2841x over previous
import numpy as np
import ml_dtypes
from contextlib import ExitStack

import concourse.bass as bass
import concourse.mybir as mybir
from concourse.bass_utils import run_bass_kernel_spmd

F32 = mybir.dt.float32
BF16 = mybir.dt.bfloat16
AF = mybir.ActivationFunctionType
ALU = mybir.AluOpType

P = 128
NB = 8
L = 4096
CDIM = 1024
H = 16
DH = 64
S = 512
NT = S // P
CB = CDIM // P
PRS = 8
JTN = L // P
NCORE = 8
NGRP = 4
EXP_SHIFT = -4.0
EPS = float(np.float32(1e-5))

_start = 2.0 ** (-2.0 ** (-(np.log2(H) - 3)))
SLOPES = [float(np.float32(_start * _start ** h)) for h in range(H)]
QSCALE = [1.0 / (8.0 * s) for s in SLOPES]

ENGINES = ("sync", "gpsimd", "tensor", "vector", "scalar")


class Bld:

    def __init__(self):
        self.streams = {e: [] for e in ENGINES}
        self.cnt = {}
        self.lastwait = {e: {} for e in ENGINES}
        self.sems = {}

    def inc(self, sem, n=1):
        self.cnt[sem] = self.cnt.get(sem, 0) + n
        return self.cnt[sem]

    def val(self, sem):
        return self.cnt.get(sem, 0)

    def em(self, e, fn):
        self.streams[e].append(fn)

    def wait(self, e, sem, v):
        if not v or v <= 0:
            return
        lw = self.lastwait[e]
        if lw.get(sem, 0) >= v:
            return
        lw[sem] = v
        self.em(e, lambda eng, s=sem, vv=v: eng.wait_ge(self.sems[s], vv))

    def op(self, e, sem, fn):
        v = self.inc(sem)
        self.em(e, lambda eng, f=fn, s=sem: f(eng).then_inc(self.sems[s], 1))
        return v

    def dma(self, e, sem, out_ap, in_ap):
        v = self.inc(sem, 16)
        self.em(e, lambda eng, o=out_ap, i=in_ap, s=sem:
                eng.dma_start(out=o, in_=i).then_inc(self.sems[s], 16))
        return v


def build_nc():
    nc = bass.Bass("TRN2", target_bir_lowering=False, debug=False)
    b = Bld()

    xs = nc.dram_tensor("xs", [NB, S, CDIM], F32, kind="ExternalInput").ap()
    win = {}
    for name in ("wqT", "wkT", "wvT", "woT", "wfT", "ufT", "wiT", "uiT",
                 "wcT", "ucT", "fwoT", "fuoT"):
        win[name] = nc.dram_tensor(name, [CDIM, CDIM], F32, kind="ExternalInput").ap()
    bias_in = {nm: nc.dram_tensor(nm, [P, CB], F32, kind="ExternalInput").ap()
               for nm in ("boat", "bfr", "bir", "bcr", "bor", "incr")}
    dtab_in = nc.dram_tensor("dtab", [P, JTN * S], BF16, kind="ExternalInput").ap()
    idbf_in = nc.dram_tensor("idbf", [P, P], BF16, kind="ExternalInput").ap()
    idf_in = nc.dram_tensor("idf", [P, P], F32, kind="ExternalInput").ap()
    ones_in = nc.dram_tensor("onesr", [P, DH], F32, kind="ExternalInput").ap()
    ppq_in = nc.dram_tensor("ppq", [P, PRS], F32, kind="ExternalInput").ap()
    NCON = 24
    cns_in = nc.dram_tensor("cns", [P, NCON], F32, kind="ExternalInput").ap()
    out = nc.dram_tensor("out", [NB, S, CDIM], F32, kind="ExternalOutput").ap()

    xnt_h = nc.dram_tensor("xnt_h", [NB, P, L], BF16)
    qt_h = nc.dram_tensor("qt_h", [NB, P, L], BF16)
    kcon = nc.dram_tensor("kcon", [64, 65536], BF16)
    kg = nc.dram_tensor("kg", [NCORE * 64, 65536], BF16)
    vcon = nc.dram_tensor("vcon", [L, H * 65], BF16)
    vg = nc.dram_tensor("vg", [NCORE * L, H * 65], BF16)
    attn_h = nc.dram_tensor("attn_h", [NB, P, L], F32)

    with ExitStack() as top:
        for s_ in ("qSY", "qGP", "cc", "tEV", "vEV", "sEV", "qOUT"):
            b.sems[s_] = top.enter_context(nc.semaphore(s_))

        sb = lambda name, shp, dt: top.enter_context(nc.sbuf_tensor(name, shp, dt))

        cns_sb = sb("cns_sb", [P, NCON], F32)
        bias_sb = {nm: sb(nm + "_sb", [P, CB], F32) for nm in
                   ("boat", "bfr", "bir", "bcr", "bor", "incr")}
        ppq_sb = sb("ppq_sb", [P, PRS], F32)
        ones_sb = sb("ones_sb", [P, DH], F32)
        idbf_sb = sb("idbf_sb", [P, P], BF16)
        idf_sb = sb("idf_sb", [P, P], F32)
        cT = sb("cT", [P, CB * S], F32)
        cbf = [sb("cbf0", [P, CB * S], BF16), sb("cbf1", [P, CB * S], BF16)]

        cvals = {}

        def constv(v):
            v = float(v)
            if v not in cvals:
                i = len(cvals)
                assert i < NCON
                cvals[v] = i
                nc.const_aps.register(F32, v, cns_sb[:, i:i + 1])
            return v

        for s_ in SLOPES:
            constv(s_)
        constv(EXP_SHIFT)
        constv(EPS)
        constv(-0.5)

        pst = [top.enter_context(nc.psum_tensor("pg%d" % i, [P, 2 * S], F32))
               for i in range(4)]
        ps = [pst[i // 2][:, (i % 2) * S:(i % 2 + 1) * S] for i in range(8)]
        ps_bf = [pst[i // 2].ap().bitcast(BF16)[:, (i % 2) * 2 * S:((i % 2) * 2 + 2) * S]
                 for i in range(8)]
        avp = [pst[2][0:65, 0:S], pst[2][0:65, S:2 * S],
               pst[3][0:65, 0:S], pst[3][0:65, S:2 * S]]
        avdn = [pst[2][DH:DH + 1, 0:S], pst[2][DH:DH + 1, S:2 * S],
                pst[3][DH:DH + 1, 0:S], pst[3][DH:DH + 1, S:2 * S]]
        avrep = [pst[2][DH:P, 0:S], pst[2][DH:P, S:2 * S],
                 pst[3][DH:P, 0:S], pst[3][DH:P, S:2 * S]]

        for nm, src in [("cns", cns_in), ("ppq", ppq_in), ("onesr", ones_in),
                        ("idbf", idbf_in), ("idf", idf_in)]:
            dst = {"cns": cns_sb, "ppq": ppq_sb, "onesr": ones_sb,
                   "idbf": idbf_sb, "idf": idf_sb}[nm]
            b.dma("sync", "qSY", dst[:, :], src)
        for nm in bias_sb:
            b.dma("sync", "qSY", bias_sb[nm][:, :], bias_in[nm])
        pre_ld = b.val("qSY")
        for e in ENGINES:
            b.wait(e, "qSY", pre_ld)

        with ExitStack() as p0:
            s0 = lambda name, shp, dt: p0.enter_context(nc.sbuf_tensor(name, shp, dt))
            wq_sb = s0("wq_sb", [P, CB * CDIM], BF16)
            wk_sb = s0("wk_sb", [P, CB * CDIM], BF16)
            wv_sb = s0("wv_sb", [P, CB * CDIM], BF16)
            xnt_sb = [s0("xnt0", [P, L], BF16), s0("xnt1", [P, L], BF16)]
            kc_sb = s0("kc_sb", [P, L], BF16)
            qt_sb = s0("qt_sb", [P, L], BF16)
            vc_sb = [s0("vc0", [P, H * 65], BF16), s0("vc1", [P, H * 65], BF16)]
            xt_sb = [s0("xt0", [P, CDIM], F32), s0("xt1", [P, CDIM], F32)]
            xnb_sb = [s0("xnb0", [P, CDIM], BF16), s0("xnb1", [P, CDIM], BF16)]
            st_sb = s0("st_sb", [P, 12], F32)
            ag_sb = s0("ag_sb", [P, 2], F32)
            nm_sb = s0("nm_sb", [P, 1], F32)
            rs_sb = s0("rs_sb", [P, 1], F32)

            for dst, src in ((wq_sb, win["wqT"]), (wk_sb, win["wkT"]), (wv_sb, win["wvT"])):
                b.dma("gpsimd", "qGP", dst.rearrange("p (cb d) -> p cb d", d=CDIM),
                      src.rearrange("(cb p) d -> p cb d", p=P))
            wqkv_ld = b.val("qGP")

            ones_ev = 0
            for buf in vc_sb:
                ap = buf.rearrange("p (h d) -> p h d", d=65)[:, :, 64:65]
                ones_ev = b.op("vector", "vEV", lambda v, a=ap: v.memset(a, 1.0))

            ln_done = {}
            tp_last = {}
            tp_bank = {}
            xn_all = {}
            kbank = {}
            kcq_dma = {}
            xnt_dma = {}
            vdma = {}
            prod_end = {}
            prev_s1 = 0

            for t in range(NB):
                par = t % 2
                for tt in range(NT):
                    g = t * NT + tt
                    tp = g % 2
                    b.wait("sync", "vEV", ln_done.get(g - 2, 0))
                    xld = b.dma("sync", "qSY", xt_sb[tp][:, :], xs[t, tt * P:(tt + 1) * P, :])
                    b.wait("vector", "qSY", xld)
                    v1 = b.op("vector", "vEV", lambda v, x=xt_sb[tp]: v.bn_stats(st_sb[:, 0:6], x[:, 0:512]))
                    v2 = b.op("vector", "vEV", lambda v, x=xt_sb[tp]: v.bn_stats(st_sb[:, 6:12], x[:, 512:1024]))
                    b.wait("vector", "vEV", v2)
                    b.wait("vector", "sEV", prev_s1)
                    v3 = b.op("vector", "vEV", lambda v: v.bn_aggr(ag_sb[:, 0:2], st_sb[:, 0:12]))
                    b.wait("vector", "vEV", v3)
                    v4 = b.op("vector", "vEV", lambda v: v.tensor_scalar(nm_sb[:, :], ag_sb[:, 0:1], -1.0, None, ALU.mult))
                    b.wait("scalar", "vEV", v3)
                    s1 = b.op("scalar", "sEV", lambda s: s.activation(rs_sb[:, :], ag_sb[:, 1:2], AF.Ln, bias=constv(EPS)))
                    prev_s1 = s1
                    b.wait("scalar", "sEV", s1)
                    s2 = b.op("scalar", "sEV", lambda s: s.activation(rs_sb[:, :], rs_sb[:, :], AF.Exp, scale=constv(-0.5)))
                    b.wait("vector", "sEV", s2)
                    b.wait("vector", "vEV", v4)
                    b.wait("vector", "tEV", tp_last.get(g - 2, 0))
                    ln_done[g] = b.op("vector", "vEV",
                                      lambda v, x=xt_sb[tp], o=xnb_sb[tp]:
                                      v.tensor_scalar(o[:, :], x[:, :], nm_sb[:, :], rs_sb[:, :], ALU.add, ALU.mult))
                    for ci in range(CB):
                        bfv = ps_bf[4 + (ci % 2)]
                        b.wait("tensor", "vEV", ln_done[g])
                        b.wait("tensor", "vEV", tp_bank.get(ci % 2, 0))
                        tpv = b.op("tensor", "tEV",
                                   lambda te, src=xnb_sb[tp], c=ci, ov=bfv:
                                   te.transpose(ov[:, 0:P], src[:, c * P:(c + 1) * P], idbf_sb[:, :]))
                        b.wait("vector", "tEV", tpv)
                        ev = b.op("vector", "vEV",
                                  lambda v, ov=bfv, dst=xnt_sb[par], c=ci, tti=tt:
                                  v.tensor_copy(dst[:, c * S + tti * P:c * S + (tti + 1) * P], ov[:, 0:P]))
                        tp_bank[ci % 2] = ev
                        tp_last[g] = tpv
                    xn_all[t] = ev

                b.wait("sync", "vEV", xn_all[t])
                xnt_dma[t] = b.dma("sync", "qSY", xnt_h.ap()[t], xnt_sb[par][:, :])

                b.wait("tensor", "qGP", wqkv_ld)
                last_ev = 0
                for which in ("k", "q"):
                    wsb = wk_sb if which == "k" else wq_sb
                    for pr in range(PRS):
                        bank = ps[2 + (pr % 2)]
                        b.wait("tensor", "vEV", kbank.get(pr % 2, 0))
                        b.wait("tensor", "vEV", xn_all[t])
                        if which == "k" and pr == 0:
                            b.wait("vector", "qSY", kcq_dma.get(t - 1, 0))
                        for ci in range(CB):
                            tv = b.op("tensor", "tEV",
                                      lambda te, w=wsb, pp=pr, c=ci, bk=bank, xp=xnt_sb[par]:
                                      te.matmul(bk, w[:, c * CDIM + pp * P:c * CDIM + (pp + 1) * P],
                                                xp[:, c * S:(c + 1) * S],
                                                start=(c == 0), stop=(c == CB - 1)))
                        b.wait("vector", "tEV", tv)
                        if which == "k":
                            ev = b.op("vector", "vEV",
                                      lambda v, bk=bank, pp=pr: v.tensor_copy(kc_sb[:, pp * S:(pp + 1) * S], bk))
                        else:
                            ev = b.op("vector", "vEV",
                                      lambda v, bk=bank, pp=pr:
                                      v.tensor_scalar(qt_sb[:, pp * S:(pp + 1) * S], bk,
                                                      ppq_sb[:, pp:pp + 1], None, ALU.mult))
                        kbank[pr % 2] = ev
                        last_ev = ev

                b.wait("sync", "vEV", last_ev)
                b.dma("sync", "qSY", qt_h.ap()[t], qt_sb[:, :])
                kview = kcon.rearrange("r (p h j) -> r p h j", p=DH, h=PRS)
                for jtl in range(NT):
                    for hpar in range(2):
                        row = t * 8 + jtl * 2 + hpar
                        src = kc_sb[hpar * DH:(hpar + 1) * DH, :]\
                            .rearrange("p (h tok) -> p h tok", tok=S)[:, :, jtl * P:(jtl + 1) * P]
                        b.dma("sync", "qSY", kview[row], src)
                kcq_dma[t] = b.val("qSY")

                b.wait("vector", "vEV", ones_ev)
                for tt in range(NT):
                    gv = t * NT + tt
                    vpar = gv % 2
                    for dc in range(2):
                        bank = ps[2 + dc]
                        b.wait("tensor", "vEV", kbank.get(dc, 0))
                        if dc == 0:
                            b.wait("tensor", "qSY", vdma.get(gv - 2, 0))
                        for ci in range(CB):
                            tv = b.op("tensor", "tEV",
                                      lambda te, c=ci, tti=tt, dci=dc, bk=bank, xp=xnt_sb[par]:
                                      te.matmul(bk,
                                                xp[:, c * S + tti * P:c * S + (tti + 1) * P],
                                                wv_sb[:, c * CDIM + dci * S:c * CDIM + (dci + 1) * S],
                                                start=(c == 0), stop=(c == CB - 1)))
                        b.wait("vector", "tEV", tv)
                        b.wait("vector", "qSY", vdma.get(gv - 2, 0))
                        dstv = vc_sb[vpar].rearrange("p (h d) -> p h d", d=65)[:, dc * 8:(dc + 1) * 8, 0:DH]
                        srcv = bank.rearrange("p (h d) -> p h d", d=DH)
                        ev = b.op("vector", "vEV", lambda v, dd=dstv, ss=srcv: v.tensor_copy(dd, ss))
                        kbank[dc] = ev
                    b.wait("sync", "vEV", ev)
                    vdma[gv] = b.dma("sync", "qSY",
                                     vcon.ap()[t * S + tt * P:t * S + (tt + 1) * P, :],
                                     vc_sb[vpar][:, :])
                prod_end[t] = b.val("tEV")
            p0_t, p0_v, p0_s = b.val("tEV"), b.val("vEV"), b.val("sEV")
            p0_sy, p0_gp = b.val("qSY"), b.val("qGP")

        b.wait("gpsimd", "qSY", p0_sy)
        for con, gat in ((kcon, kg), (vcon, vg)):
            b.inc("cc", 1)
            b.em("gpsimd", lambda g, ci=con, go=gat: g.collective_compute(
                "AllGather", ALU.bypass, replica_groups=[list(range(NCORE))],
                ins=[ci.ap().opt()], outs=[go.ap().opt()]).then_inc(b.sems["cc"], 1))
        cc_done = b.val("cc")

        with ExitStack() as p1:
            s1_ = lambda name, shp, dt: p1.enter_context(nc.sbuf_tensor(name, shp, dt))
            wg_sb = {nm: s1_(nm + "_sb", [P, CB * CDIM], BF16)
                     for nm in ("wfT", "ufT", "wiT", "uiT", "wcT", "ucT")}
            xnf_sb = [s1_("xnf0", [P, L], BF16), s1_("xnf1", [P, L], BF16)]
            f_sb = s1_("f_sb", [P, S], F32)
            i_sb = s1_("i_sb", [P, S], F32)
            g_sb = s1_("g_sb", [P, S], F32)
            t1_sb = s1_("t1_sb", [P, S], F32)
            t2_sb = s1_("t2_sb", [P, S], F32)

            for qe in ("sync", "gpsimd"):
                b.wait(qe, "tEV", p0_t)
                b.wait(qe, "vEV", p0_v)
                b.wait(qe, "qSY", p0_sy)
                b.wait(qe, "qGP", p0_gp)
            for qe in ("tensor", "vector", "scalar"):
                b.wait(qe, "tEV", p0_t)
                b.wait(qe, "vEV", p0_v)
                b.wait(qe, "sEV", p0_s)
            for nm in ("wfT", "ufT", "wiT", "uiT", "wcT", "ucT"):
                b.dma("gpsimd", "qGP", wg_sb[nm].rearrange("p (cb d) -> p cb d", d=CDIM),
                      win[nm].rearrange("(cb p) d -> p cb d", p=P))
            wffn_ld = b.val("qGP")
            xnf_ld = {}
            b.wait("sync", "qSY", xnt_dma[0])
            xnf_ld[0] = b.dma("sync", "qSY", xnf_sb[0][:, :], xnt_h.ap()[0])

            vz = b.op("vector", "vEV", lambda v: v.memset(cT[:, :], 0.0))
            b.wait("scalar", "vEV", vz)
            c0e = 0
            for mt in range(CB):
                c0e = b.op("scalar", "sEV",
                           lambda s, m=mt: s.activation(cT[:, m * S:(m + 1) * S], cT[:, m * S:(m + 1) * S],
                                                        AF.Identity, bias=bias_sb["incr"][:, m:m + 1]))
            b.wait("vector", "sEV", c0e)
            cb0 = b.op("vector", "vEV", lambda v: v.tensor_copy(cbf[0][:, :], cT[:, :]))

            gate_specs = [("wfT", "ufT", "bfr", AF.Sigmoid, f_sb),
                          ("wiT", "uiT", "bir", AF.Sigmoid, i_sb),
                          ("wcT", "ucT", "bcr", AF.Tanh, g_sb)]
            gact = {}
            gbuf_used = {}
            ffn_t_end = {}
            last_cupd = cb0
            for t in range(NB):
                rpar, wpar = t % 2, (t + 1) % 2
                if t + 1 < NB:
                    b.wait("sync", "tEV", ffn_t_end.get(t - 1, 0))
                    b.wait("sync", "qSY", xnt_dma[t + 1])
                    xnf_ld[t + 1] = b.dma("sync", "qSY", xnf_sb[wpar][:, :], xnt_h.ap()[t + 1])
                for mt in range(CB):
                    gevs = []
                    for gi, (wn, un, bn, fn, gbuf) in enumerate(gate_specs):
                        bank = ps[gi]
                        b.wait("tensor", "qGP", wffn_ld)
                        b.wait("tensor", "qSY", xnf_ld[t])
                        b.wait("tensor", "vEV", last_cupd)
                        b.wait("tensor", "sEV", gact.get(gi, 0))
                        for ci in range(CB):
                            b.op("tensor", "tEV",
                                 lambda te, w=wg_sb[wn], m=mt, c=ci, bk=bank, xp=xnf_sb[rpar]:
                                 te.matmul(bk, w[:, c * CDIM + m * P:c * CDIM + (m + 1) * P],
                                           xp[:, c * S:(c + 1) * S], start=(c == 0), stop=False))
                        for ci in range(CB):
                            tv = b.op("tensor", "tEV",
                                      lambda te, u=wg_sb[un], m=mt, c=ci, bk=bank, cc_=cbf[rpar]:
                                      te.matmul(bk, u[:, c * CDIM + m * P:c * CDIM + (m + 1) * P],
                                                cc_[:, c * S:(c + 1) * S], start=False, stop=(c == CB - 1)))
                        b.wait("scalar", "tEV", tv)
                        b.wait("scalar", "vEV", gbuf_used.get(gi, 0))
                        sv = b.op("scalar", "sEV",
                                  lambda s, bk=bank, f=fn, bb=bn, m=mt, gb=gbuf:
                                  s.activation(gb[:, :], bk, f, bias=bias_sb[bb][:, m:m + 1]))
                        gact[gi] = sv
                        gevs.append(sv)
                    b.wait("vector", "sEV", gevs[0])
                    u1 = b.op("vector", "vEV",
                              lambda v, m=mt: v.tensor_tensor(t1_sb[:, :], f_sb[:, :], cT[:, m * S:(m + 1) * S], ALU.mult))
                    b.wait("vector", "sEV", gevs[2])
                    u2 = b.op("vector", "vEV",
                              lambda v: v.tensor_tensor(t2_sb[:, :], i_sb[:, :], g_sb[:, :], ALU.mult))
                    b.wait("vector", "vEV", u2)
                    u3 = b.op("vector", "vEV",
                              lambda v, m=mt: v.tensor_tensor(cT[:, m * S:(m + 1) * S], t1_sb[:, :], t2_sb[:, :], ALU.add))
                    b.wait("vector", "vEV", u3)
                    u4 = b.op("vector", "vEV",
                              lambda v, m=mt, wp=wpar: v.tensor_copy(cbf[wp][:, m * S:(m + 1) * S],
                                                                     cT[:, m * S:(m + 1) * S]))
                    gbuf_used[0] = u1
                    gbuf_used[1] = u2
                    gbuf_used[2] = u2
                    last_cupd = u4
                ffn_t_end[t] = b.val("tEV")
            cbf_fin = cbf[NB % 2]
            p1_t, p1_v, p1_s = b.val("tEV"), b.val("vEV"), b.val("sEV")
            p1_sy, p1_gp = b.val("qSY"), b.val("qGP")

        KWARM = 0
        with ExitStack() as p2:
            s2_ = lambda name, shp, dt: p2.enter_context(nc.sbuf_tensor(name, shp, dt))
            dtab_sb = s2_("dtab_sb", [P, JTN * S], BF16)
            wo64_sb = s2_("wo64_sb", [DH, H * CDIM], BF16)
            KTb = [s2_("KT0", [P, 16 * 256], BF16), s2_("KT1", [P, 16 * 256], BF16)]
            VTb = [s2_("VT0", [P, 16 * 260], BF16), s2_("VT1", [P, 16 * 260], BF16)]
            QT = [s2_("QT0", [P, L], BF16)]
            tsub = [s2_("ts%d" % i, [P, S], F32) for i in range(4)]
            Pt = [s2_("Pt%d" % i, [P, S], BF16) for i in range(4)]
            AVT = [s2_("AVT0", [DH, H * S], BF16)]
            norm_sb = s2_("norm_sb", [P, S], F32)
            rep_sb = s2_("rep_sb", [DH, S], F32)
            attn_sb = [s2_("at0", [P, L], F32)]

            for qe in ("sync", "gpsimd"):
                b.wait(qe, "tEV", p1_t)
                b.wait(qe, "vEV", p1_v)
                b.wait(qe, "sEV", p1_s)
                b.wait(qe, "qSY", p1_sy)
                b.wait(qe, "qGP", p1_gp)
            for qe in ("tensor", "vector", "scalar"):
                b.wait(qe, "tEV", p1_t)
                b.wait(qe, "vEV", p1_v)
                b.wait(qe, "sEV", p1_s)
            b.wait("sync", "cc", cc_done)
            dtab_ld = b.dma("sync", "qSY", dtab_sb[:, :], dtab_in)
            for h in range(H):
                b.dma("gpsimd", "qGP", wo64_sb[:, h * CDIM:(h + 1) * CDIM],
                      win["woT"][h * DH:(h + 1) * DH, :])
            wo_ld = b.val("qGP")

            kgr = kg.rearrange("(c bb jl hp) (p x) -> c bb jl hp p x", c=NCORE, bb=NB, jl=NT, p=DH)
            vgr = vg.rearrange("(c bb jl p) x -> c bb jl p x", c=NCORE, bb=NB, jl=NT)

            def load_kv(bb, hg, half, buf):
                for hpar in range(2):
                    for cl in range(4):
                        core = 4 * half + cl
                        src = kgr[core, bb, :, hpar, :, 2 * hg * P:2 * hg * P + 256]\
                            .rearrange("jl p x -> p jl x")
                        dst = KTb[buf][hpar * DH:(hpar + 1) * DH, cl * 1024:(cl + 1) * 1024]\
                            .rearrange("p (jl x) -> p jl x", x=256)
                        b.dma("sync", "qSY", dst, src)
                for cl in range(4):
                    core = 4 * half + cl
                    src = vgr[core, bb, :, :, 4 * hg * 65:(4 * hg + 4) * 65]\
                        .rearrange("jl p x -> p jl x")
                    dst = VTb[buf][:, cl * 1040:(cl + 1) * 1040]\
                        .rearrange("p (jl x) -> p jl x", x=260)
                    b.dma("sync", "qSY", dst, src)
                return b.val("qSY")

            sub_done = [0, 0, 0, 0]
            exp_done = [0, 0, 0, 0]
            av_done = [0, 0, 0, 0]
            kv_free = [0, 0]
            kv_ld = {}
            avt_mult = {}
            rep_state = {"rpm": 0, "evac": 0, "mul": 0}
            st_lastq = {}
            proj_done = {}
            attn_dma = {}
            act_bank = {}
            act_last = {}
            nstep = 0
            pend = []
            pending_norm = []

            def flush_av(limit):
                while len(pend) > limit:
                    par, slot, jt, jl2, buf, sexp, hloc = pend.pop(0)
                    b.wait("tensor", "sEV", sexp)
                    if jt == 0:
                        b.wait("tensor", "vEV", avt_mult.get(slot, 0))
                    v_av = b.op("tensor", "tEV",
                                lambda te, pp=par, sl=slot, bf_=buf, jj=jl2, hh=hloc,
                                first=(jt == 0), last=(jt == JTN - 1):
                                te.matmul(avp[sl],
                                          VTb[bf_][:, jj * 260 + hh * 65:jj * 260 + hh * 65 + 65],
                                          Pt[pp][:, :], start=first, stop=last))
                    av_done[par] = v_av
                    kv_free[buf] = v_av

            def emit_norm():
                while pending_norm:
                    sgi, hl2, h, slot = pending_norm.pop(0)
                    b.wait("vector", "tEV", norm_av_done[slot])
                    b.wait("vector", "tEV", rep_state["rpm"])
                    rcp = b.op("vector", "vEV",
                               lambda v, sl=slot: v.reciprocal(norm_sb[DH:DH + 1, :], avdn[sl]))
                    b.wait("tensor", "vEV", rcp)
                    b.wait("tensor", "vEV", rep_state["evac"])
                    rpm = b.op("tensor", "tEV",
                               lambda te, sl=slot: te.matmul(avrep[sl], ones_sb[DH:DH + 1, 0:DH],
                                                             norm_sb[DH:DH + 1, :], start=True, stop=True,
                                                             tile_position=(DH, DH)))
                    rep_state["rpm"] = rpm
                    b.wait("vector", "tEV", rpm)
                    b.wait("vector", "vEV", rep_state["mul"])
                    rpe = b.op("vector", "vEV", lambda v, sl=slot: v.tensor_copy(rep_sb[:, :], avrep[sl]))
                    rep_state["evac"] = rpe
                    b.wait("vector", "vEV", rpe)
                    b.wait("vector", "tEV", proj_done.get(norm_bb[slot] - 1, 0))
                    mul = b.op("vector", "vEV",
                               lambda v, sl=slot, hh=h:
                               v.tensor_tensor(AVT[0][:, hh * S:(hh + 1) * S],
                                               avp[sl][0:DH, :], rep_sb[:, :], ALU.mult))
                    rep_state["mul"] = mul
                    avt_mult[slot] = mul

            norm_av_done = [0, 0, 0, 0]
            norm_bb = [0, 0, 0, 0]
            NCHUNK = NB * NGRP * 2
            sgidx = 0
            for bb in range(NB):
                b.wait("sync", "tEV", st_lastq.get(bb - 1, 0))
                qld = b.dma("sync", "qSY", QT[0][:, :], qt_h.ap()[bb])
                for hg in range(NGRP):
                    g2 = bb * NGRP + hg
                    if g2 == 0:
                        kv_ld[0] = load_kv(0, 0, 0, 0)
                    for sg in range(2):
                        aset = sgidx % 2
                        for jt in range(JTN):
                            half = jt // 16
                            jl2 = jt % 16
                            c2 = 2 * g2 + half
                            if (sg == 0 and jt == 1) or (sg == 1 and jt == 17):
                                nxt = 2 * g2 + (1 if sg == 0 else 2)
                                if nxt < NCHUNK and nxt not in kv_ld:
                                    cbb, rem = divmod(nxt, 2 * NGRP)
                                    chg, chalf = divmod(rem, 2)
                                    b.wait("sync", "tEV", kv_free[nxt % 2])
                                    kv_ld[nxt] = load_kv(cbb, chg, chalf, nxt % 2)
                            if jt == 2:
                                emit_norm()
                            if KWARM and jt >= 3 and jt % KWARM == 0:
                                idle = 1 - aset
                                b.wait("tensor", "vEV", max(avt_mult.get(2 * idle, 0),
                                                            avt_mult.get(2 * idle + 1, 0)))
                                b.op("tensor", "tEV",
                                     lambda te, ii=idle: te.matmul(pst[2 + ii][:, 0:S], idbf_sb[:, :],
                                                                   QT[0][:, 0:S], start=True, stop=True))
                            stv = []
                            for hl2 in range(2):
                                h = hg * 4 + 2 * sg + hl2
                                par = nstep % 4
                                b.wait("tensor", "vEV", sub_done[par])
                                if par < 2:
                                    b.wait("tensor", "sEV", act_bank.get(par, 0))
                                b.wait("tensor", "qSY", kv_ld[c2])
                                b.wait("tensor", "qSY", qld)
                                v_st = b.op("tensor", "tEV",
                                            lambda te, pp=par, hp=hl2, hh=h, jj=jl2, bf_=c2 % 2, sgo=sg:
                                            te.matmul(ps[pp],
                                                      KTb[bf_][hp * DH:(hp + 1) * DH,
                                                               jj * 256 + sgo * P:jj * 256 + (sgo + 1) * P],
                                                      QT[0][hp * DH:(hp + 1) * DH, (hh // 2) * S:(hh // 2 + 1) * S],
                                                      start=True, stop=True, tile_position=(hp * DH, 0)))
                                stv.append((v_st, par, h, hl2))
                                nstep += 1
                            flush_av(2)
                            for v_st, par, h, hl2 in stv:
                                b.wait("vector", "tEV", v_st)
                                b.wait("vector", "sEV", exp_done[par])
                                b.wait("vector", "qSY", dtab_ld)
                                v_sub = b.op("vector", "vEV",
                                             lambda v, pp=par, jj=jt:
                                             v.tensor_tensor(tsub[pp][:, :], ps[pp],
                                                             dtab_sb[:, jj * S:(jj + 1) * S], ALU.subtract))
                                sub_done[par] = v_sub
                                b.wait("scalar", "vEV", v_sub)
                                b.wait("scalar", "tEV", av_done[par])
                                v_exp = b.op("scalar", "sEV",
                                             lambda s, pp=par, hh=h:
                                             s.activation(Pt[pp][:, :], tsub[pp][:, :], AF.Exp,
                                                          bias=constv(EXP_SHIFT), scale=constv(SLOPES[hh])))
                                exp_done[par] = v_exp
                                slot = 2 * aset + hl2
                                pend.append((par, slot, jt, jl2, c2 % 2, v_exp, 2 * sg + hl2))
                        flush_av(0)
                        for hl2 in range(2):
                            slot = 2 * aset + hl2
                            norm_av_done[slot] = b.val("tEV")
                            norm_bb[slot] = bb
                            pending_norm.append((sgidx, hl2, hg * 4 + 2 * sg + hl2, slot))
                        sgidx += 1
                st_lastq[bb] = b.val("tEV")
                emit_norm()
                allmul = b.val("vEV")
                for ot in range(CB):
                    bank = ps[ot % 2]
                    b.wait("tensor", "vEV", allmul)
                    b.wait("tensor", "qGP", wo_ld)
                    b.wait("tensor", "sEV", act_bank.get(ot % 2, 0))
                    for h in range(H):
                        tv = b.op("tensor", "tEV",
                                  lambda te, hh=h, o=ot, bk=bank:
                                  te.matmul(bk, wo64_sb[:, hh * CDIM + o * P:hh * CDIM + (o + 1) * P],
                                            AVT[0][:, hh * S:(hh + 1) * S],
                                            start=(hh == 0), stop=(hh == H - 1)))
                    b.wait("scalar", "tEV", tv)
                    b.wait("scalar", "qSY", attn_dma.get(bb - 1, 0))
                    sv = b.op("scalar", "sEV",
                              lambda s, bk=bank, o=ot:
                              s.activation(attn_sb[0][:, o * S:(o + 1) * S], bk,
                                           AF.Identity, bias=bias_sb["boat"][:, o:o + 1]))
                    act_bank[ot % 2] = sv
                proj_done[bb] = b.val("tEV")
                act_last[bb] = b.val("sEV")
                b.wait("sync", "sEV", act_last[bb])
                attn_dma[bb] = b.dma("sync", "qSY", attn_h.ap()[bb], attn_sb[0][:, :])
            p2_t, p2_v, p2_s = b.val("tEV"), b.val("vEV"), b.val("sEV")
            p2_sy, p2_gp = b.val("qSY"), b.val("qGP")

        with ExitStack() as p3:
            s3_ = lambda name, shp, dt: p3.enter_context(nc.sbuf_tensor(name, shp, dt))
            wo_sb = s3_("wo_sb", [P, CB * CDIM], BF16)
            uo_sb = s3_("uo_sb", [P, CB * CDIM], BF16)
            uoC_sb = s3_("uoC_sb", [P, CB * S], F32)
            xnf2 = [s3_("xnf2a", [P, L], BF16), s3_("xnf2b", [P, L], BF16)]
            att2 = [s3_("att2a", [P, L], F32), s3_("att2b", [P, L], F32)]
            t3_sb = [s3_("t3a", [P, S], F32), s3_("t3b", [P, S], F32)]
            ht_sb = [s3_("hta", [P, S], F32), s3_("htb", [P, S], F32)]
            sum_sb = [s3_("suma", [P, L], F32), s3_("sumb", [P, L], F32)]
            xnat = [s3_("xna", [P, CDIM], F32), s3_("xnb2", [P, CDIM], F32)]
            onat = [s3_("ona", [P, CDIM], F32), s3_("onb", [P, CDIM], F32)]

            for qe in ("sync", "gpsimd"):
                b.wait(qe, "tEV", p2_t)
                b.wait(qe, "vEV", p2_v)
                b.wait(qe, "sEV", p2_s)
                b.wait(qe, "qSY", p2_sy)
                b.wait(qe, "qGP", p2_gp)
            for qe in ("tensor", "vector", "scalar"):
                b.wait(qe, "tEV", p2_t)
                b.wait(qe, "vEV", p2_v)
                b.wait(qe, "sEV", p2_s)
            b.dma("gpsimd", "qGP", wo_sb.rearrange("p (cb d) -> p cb d", d=CDIM),
                  win["fwoT"].rearrange("(cb p) d -> p cb d", p=P))
            b.dma("gpsimd", "qGP", uo_sb.rearrange("p (cb d) -> p cb d", d=CDIM),
                  win["fuoT"].rearrange("(cb p) d -> p cb d", p=P))
            wP3_ld = b.val("qGP")

            uoC_ev = {}
            for mt in range(CB):
                bank = ps[mt % 2]
                b.wait("tensor", "qGP", wP3_ld)
                b.wait("tensor", "vEV", uoC_ev.get(mt - 2, 0))
                for ci in range(CB):
                    tv = b.op("tensor", "tEV",
                              lambda te, m=mt, c=ci, bk=bank:
                              te.matmul(bk, uo_sb[:, c * CDIM + m * P:c * CDIM + (m + 1) * P],
                                        cbf_fin[:, c * S:(c + 1) * S],
                                        start=(c == 0), stop=(c == CB - 1)))
                b.wait("vector", "tEV", tv)
                uoC_ev[mt] = b.op("vector", "vEV",
                                  lambda v, m=mt, bk=bank: v.tensor_copy(uoC_sb[:, m * S:(m + 1) * S], bk))
            uoC_all = uoC_ev[CB - 1]

            ld3 = {}
            sum_ev = {}
            t3_used = {}
            ht_used = {}
            wo_bank = {}
            tpo_done = {}
            out_dma = {}
            onat_ev = {}
            ob_bank = {}
            b.wait("sync", "qSY", xnt_dma[0])
            b.wait("sync", "qSY", attn_dma[0])
            ld3[0] = b.dma("sync", "qSY", att2[0][:, :], attn_h.ap()[0])
            b.dma("sync", "qSY", xnf2[0][:, :], xnt_h.ap()[0])
            ld3[0] = b.val("qSY")
            for t in range(NB):
                tp = t % 2
                if t + 1 < NB:
                    b.wait("sync", "tEV", tpo_done.get(t - 1, 0))
                    b.wait("sync", "vEV", sum_ev.get((t - 1, CB - 1), 0))
                    b.wait("sync", "qSY", attn_dma[t + 1])
                    b.dma("sync", "qSY", att2[(t + 1) % 2][:, :], attn_h.ap()[t + 1])
                    b.dma("sync", "qSY", xnf2[(t + 1) % 2][:, :], xnt_h.ap()[t + 1])
                    ld3[t + 1] = b.val("qSY")
                for mt in range(CB):
                    bank = ps[mt % 2]
                    b.wait("tensor", "qSY", ld3[t])
                    b.wait("tensor", "vEV", uoC_all)
                    b.wait("tensor", "vEV", wo_bank.get(mt % 2, 0))
                    for ci in range(CB):
                        tv = b.op("tensor", "tEV",
                                  lambda te, m=mt, c=ci, bk=bank, xp=xnf2[tp]:
                                  te.matmul(bk, wo_sb[:, c * CDIM + m * P:c * CDIM + (m + 1) * P],
                                            xp[:, c * S:(c + 1) * S],
                                            start=(c == 0), stop=(c == CB - 1)))
                    b.wait("vector", "tEV", tv)
                    b.wait("vector", "sEV", t3_used.get(mt % 2, 0))
                    v5 = b.op("vector", "vEV",
                              lambda v, m=mt, bk=bank, pp=mt % 2:
                              v.scalar_tensor_tensor(t3_sb[pp][:, :], bk, 1.0,
                                                     uoC_sb[:, m * S:(m + 1) * S], ALU.mult, ALU.add))
                    wo_bank[mt % 2] = v5
                    b.wait("scalar", "vEV", v5)
                    b.wait("scalar", "vEV", ht_used.get(mt % 2, 0))
                    s5 = b.op("scalar", "sEV",
                              lambda s, m=mt, pp=mt % 2:
                              s.activation(ht_sb[pp][:, :], t3_sb[pp][:, :], AF.Tanh,
                                           bias=bias_sb["bor"][:, m:m + 1]))
                    t3_used[mt % 2] = s5
                    b.wait("vector", "sEV", s5)
                    b.wait("vector", "qSY", ld3[t])
                    b.wait("vector", "tEV", tpo_done.get(t - 2, 0))
                    v6 = b.op("vector", "vEV",
                              lambda v, m=mt, pp=mt % 2, ap_=att2[tp], ss=sum_sb[tp]:
                              v.tensor_tensor(ss[:, m * S:(m + 1) * S], ht_sb[pp][:, :],
                                              ap_[:, m * S:(m + 1) * S], ALU.add))
                    ht_used[mt % 2] = v6
                    sum_ev[(t, mt)] = v6
                for tt in range(NT):
                    g = t * NT + tt
                    ttp = g % 2
                    b.wait("sync", "vEV", onat_ev.get(g - 2, 0))
                    xld3 = b.dma("sync", "qSY", xnat[ttp][:, :], xs[t, tt * P:(tt + 1) * P, :])
                    for ot in range(CB):
                        bank = ps[2 + (ot % 2)]
                        b.wait("tensor", "vEV", sum_ev[(t, ot)])
                        b.wait("tensor", "vEV", ob_bank.get(ot % 2, 0))
                        tpv = b.op("tensor", "tEV",
                                   lambda te, o=ot, tti=tt, bk=bank, ss=sum_sb[tp]:
                                   te.transpose(bk[:, 0:P], ss[:, o * S + tti * P:o * S + (tti + 1) * P],
                                                idf_sb[:, :]))
                        b.wait("vector", "tEV", tpv)
                        b.wait("vector", "qSY", xld3)
                        b.wait("vector", "qOUT", out_dma.get(g - 2, 0))
                        ov = b.op("vector", "vEV",
                                  lambda v, o=ot, bk=bank, tpp=ttp:
                                  v.scalar_tensor_tensor(onat[tpp][:, o * P:(o + 1) * P], bk[:, 0:P], 1.0,
                                                         xnat[tpp][:, o * P:(o + 1) * P], ALU.mult, ALU.add))
                        ob_bank[ot % 2] = ov
                        onat_ev[g] = ov
                    tpo_done[t] = b.val("tEV")
                    b.wait("sync", "vEV", onat_ev[g])
                    out_dma[g] = b.dma("sync", "qOUT", out[t, tt * P:(tt + 1) * P, :], onat[ttp][:, :])

            b.wait("sync", "qOUT", b.val("qOUT"))

        with nc.Block() as block:
            @block.sync
            def _(eng):
                for fn in b.streams["sync"]:
                    fn(eng)

            @block.gpsimd
            def _(eng):
                for fn in b.streams["gpsimd"]:
                    fn(eng)

            @block.tensor
            def _(eng):
                for fn in b.streams["tensor"]:
                    fn(eng)

            @block.vector
            def _(eng):
                for fn in b.streams["vector"]:
                    fn(eng)

            @block.scalar
            def _(eng):
                for fn in b.streams["scalar"]:
                    fn(eng)

    return nc


def _host_consts():
    idbf = np.eye(P, dtype=ml_dtypes.bfloat16)
    idf = np.eye(P, dtype=np.float32)
    onesr = np.ones((P, DH), dtype=np.float32)
    ppq = np.zeros((P, PRS), dtype=np.float32)
    for pr in range(PRS):
        ppq[0:DH, pr] = QSCALE[2 * pr]
        ppq[DH:P, pr] = QSCALE[2 * pr + 1]
    cns = np.zeros((P, 24), dtype=np.float32)
    vals = list(SLOPES) + [EXP_SHIFT, EPS, -0.5]
    for i, v in enumerate(vals):
        cns[:, i] = np.float32(v)
    return idbf, idf, onesr, ppq, cns


def _dtab_for_core(c):
    p = np.arange(P, dtype=np.float64)[:, None, None]
    jt = np.arange(JTN, dtype=np.float64)[None, :, None]
    f = np.arange(S, dtype=np.float64)[None, None, :]
    d = np.abs((c * S + f) - (jt * P + p))
    return d.reshape(P, JTN * S).astype(ml_dtypes.bfloat16)


def _reshape_bias(v):
    return np.ascontiguousarray(v.reshape(CB, P).T.astype(np.float32))


def kernel(L0_attn_q, L0_attn_Wq, L0_attn_Wk, L0_attn_Wv, L0_attn_Wo, L0_attn_bo,
           L0_ffn_wf, L0_ffn_uf, L0_ffn_bf, L0_ffn_wi, L0_ffn_ui, L0_ffn_bi,
           L0_ffn_wc, L0_ffn_uc, L0_ffn_bc, L0_ffn_wo, L0_ffn_uo, L0_ffn_bo,
           L0_ffn_in_c):
    x = np.asarray(L0_attn_q, dtype=np.float32)
    wT = {
        "wqT": L0_attn_Wq, "wkT": L0_attn_Wk, "wvT": L0_attn_Wv, "woT": L0_attn_Wo,
        "wfT": L0_ffn_wf, "ufT": L0_ffn_uf, "wiT": L0_ffn_wi, "uiT": L0_ffn_ui,
        "wcT": L0_ffn_wc, "ucT": L0_ffn_uc, "fwoT": L0_ffn_wo, "fuoT": L0_ffn_uo,
    }
    wT = {k: np.ascontiguousarray(np.asarray(v, dtype=np.float32).T) for k, v in wT.items()}
    biases = {
        "boat": _reshape_bias(np.asarray(L0_attn_bo, np.float32)),
        "bfr": _reshape_bias(np.asarray(L0_ffn_bf, np.float32)),
        "bir": _reshape_bias(np.asarray(L0_ffn_bi, np.float32)),
        "bcr": _reshape_bias(np.asarray(L0_ffn_bc, np.float32)),
        "bor": _reshape_bias(np.asarray(L0_ffn_bo, np.float32)),
        "incr": _reshape_bias(np.asarray(L0_ffn_in_c, np.float32)),
    }
    idbf, idf, onesr, ppq, cns = _host_consts()

    nc = build_nc()
    in_maps = []
    for c in range(NCORE):
        m = {"xs": np.ascontiguousarray(x[:, c * S:(c + 1) * S, :]),
             "dtab": _dtab_for_core(c),
             "idbf": idbf, "idf": idf, "onesr": onesr, "ppq": ppq, "cns": cns}
        m.update(wT)
        m.update(biases)
        in_maps.append(m)
    res = run_bass_kernel_spmd(nc, in_maps, core_ids=list(range(NCORE)))
    outs = [np.asarray(res.results[c]["out"]) for c in range(NCORE)]
    return np.concatenate(outs, axis=1)


if __name__ == "__main__":
    nc = build_nc()
    n = sum(len(v) for v in Bld().streams.values())
    print("built OK")
